# version 06
# baseline (speedup 1.0000x reference)
import numpy as np
from contextlib import ExitStack

import concourse.bass as bass
import concourse.mybir as mybir
from concourse.bass_utils import run_bass_kernel_spmd

F32 = mybir.dt.float32
BF16 = mybir.dt.bfloat16
AX = mybir.AxisListType.X
AF = mybir.ActivationFunctionType
ALU = mybir.AluOpType

D = 2048
DC = 16
T = 1024
NE = 8
NSL = 32
NDT = 16
NH = 8
S = 2048
LAM = float(np.float32(2048.0) ** 0.5)

C_HALF, C_ONE, C_EPS, C_INVD, C_NEG1, C_NEGBIG, C_NEGSHIFT, C_S0, C_S1, C_ZERO = range(10)
NCONST = 16
SHIFT = 4.0

STAGES = ("ffn1", "attn", "ffn2")


class Sem:
    def __init__(self, nc, ctx, name):
        self.h = ctx.enter_context(nc.semaphore(name))
        self.n = 0

    def inc(self, instr, k=1):
        instr.then_inc(self.h, k)
        self.n += k
        assert self.n < 32000, f"semaphore overflow: {self.n}"
        return self.n

    def wait(self, eng, val=None):
        v = self.n if val is None else val
        if v > 0:
            eng.wait_ge(self.h, v)


def build(stages=STAGES):
    nc = bass.Bass("TRN2", target_bir_lowering=False)

    xin = nc.dram_tensor("xin", [128, DC * T], F32, kind="ExternalInput")
    win = nc.dram_tensor("win", [NE, NSL, 128, 4096], F32, kind="ExternalInput")
    wout = nc.dram_tensor("wout", [NE, NDT, 2, 128, 2048], F32, kind="ExternalInput")
    wq = nc.dram_tensor("wq", [NH, 128, 2048], F32, kind="ExternalInput")
    wk = nc.dram_tensor("wk", [NH, 128, 2048], F32, kind="ExternalInput")
    wv = nc.dram_tensor("wv", [NH, 128, 2048], F32, kind="ExternalInput")
    wo = nc.dram_tensor("wo", [NDT, 128, 2048], F32, kind="ExternalInput")
    rw = nc.dram_tensor("rw", [128, DC * 8], F32, kind="ExternalInput")
    biasH = nc.dram_tensor("biasH", [128, 16], F32, kind="ExternalInput")
    boH = nc.dram_tensor("boH", [128, 16], F32, kind="ExternalInput")
    constsH = nc.dram_tensor("constsH", [128, NCONST], F32, kind="ExternalInput")
    triH = nc.dram_tensor("triH", [128, 128], F32, kind="ExternalInput")
    identH = nc.dram_tensor("identH", [128, 128], F32, kind="ExternalInput")
    onesH = nc.dram_tensor("onesH", [128, 8], F32, kind="ExternalInput")
    out = nc.dram_tensor("out", [128, DC * T], F32, kind="ExternalOutput")

    cc_xn_in = nc.dram_tensor("cc_xn_in", [S, T], BF16, kind="Internal")
    cc_xn_outA = nc.dram_tensor("cc_xn_outA", [S, T], BF16, kind="Internal")
    cc_xn_outB = nc.dram_tensor("cc_xn_outB", [S, T], BF16, kind="Internal")
    cc_ctx_in = nc.dram_tensor("cc_ctx_in", [1024, S], BF16, kind="Internal")
    cc_ctx_outA = nc.dram_tensor("cc_ctx_outA", [1024, S], BF16, kind="Internal")
    cc_ctx_outB = nc.dram_tensor("cc_ctx_outB", [1024, S], BF16, kind="Internal")
    bcrows = nc.dram_tensor("bcrows", [8, 1024], F32, kind="Internal")

    with ExitStack() as ctx:
        e = ctx.enter_context
        rT = e(nc.sbuf_tensor("rT", [128, DC * T], F32))
        xn = e(nc.sbuf_tensor("xn", [128, DC * T], BF16))
        gT = e(nc.sbuf_tensor("gT", [128, 32 * T], BF16))
        wis = e(nc.sbuf_tensor("wis", [128, 2 * 4096], BF16))
        wos = e(nc.sbuf_tensor("wos", [128, 2 * 2048], BF16))
        sa = e(nc.sbuf_tensor("sa", [128, 1024], BF16))
        gsc = e(nc.sbuf_tensor("gsc", [128, 1024], BF16))
        gbc = e(nc.sbuf_tensor("gbc", [128, 2 * 1024], BF16))
        bigscr = e(nc.sbuf_tensor("bigscr", [128, 2048], F32))
        gatesT = e(nc.sbuf_tensor("gatesT", [8, 1024], F32))
        biasC = e(nc.sbuf_tensor("biasC", [128, 16], F32))
        bias05 = e(nc.sbuf_tensor("bias05", [128, 16], F32))
        boC = e(nc.sbuf_tensor("boC", [128, 16], F32))
        consts = e(nc.sbuf_tensor("consts", [128, NCONST], F32))
        rwT = e(nc.sbuf_tensor("rwT", [128, 128], BF16))
        tri = e(nc.sbuf_tensor("tri", [128, 128], BF16))
        idf = e(nc.sbuf_tensor("idf", [128, 128], F32))
        idb = e(nc.sbuf_tensor("idb", [128, 128], BF16))
        onesb = e(nc.sbuf_tensor("onesb", [128, 8], BF16))
        onesf = e(nc.sbuf_tensor("onesf", [128, 8], F32))
        lgt = e(nc.sbuf_tensor("lgt", [128, 8], F32))
        tsc = e(nc.sbuf_tensor("tsc", [128, 8], F32))
        tsc2 = e(nc.sbuf_tensor("tsc2", [128, 8], F32))
        cl1 = e(nc.sbuf_tensor("cl1", [128, 1], F32))
        cl2 = e(nc.sbuf_tensor("cl2", [128, 1], F32))
        cl3 = e(nc.sbuf_tensor("cl3", [128, 1], F32))
        cl4 = e(nc.sbuf_tensor("cl4", [128, 1], F32))

        pp = [e(nc.psum_tensor(f"pp{i}", [128, 1024], F32)) for i in range(4)]

        sT = Sem(nc, ctx, "sT")
        sV = Sem(nc, ctx, "sV")
        sS = Sem(nc, ctx, "sS")
        sCC = Sem(nc, ctx, "sCC")
        dS = Sem(nc, ctx, "dS")
        dG0 = Sem(nc, ctx, "dG0")
        dLN = Sem(nc, ctx, "dLN")
        dLNb = Sem(nc, ctx, "dLNb")
        dRowG = [Sem(nc, ctx, f"dRowG{i}") for i in range(2)]
        dGbc = [Sem(nc, ctx, f"dGbc{i}") for i in range(2)]
        dWin = [Sem(nc, ctx, f"dWin{i}") for i in range(2)]
        dWout = [Sem(nc, ctx, f"dWout{i}") for i in range(2)]
        dWo = [Sem(nc, ctx, f"dWo{i}") for i in range(2)]
        dAw = [Sem(nc, ctx, f"dAw{i}") for i in range(2)]
        dXn = Sem(nc, ctx, "dXn")
        dXnB = Sem(nc, ctx, "dXnB")
        dLRow = Sem(nc, ctx, "dLRow")
        dLbc = Sem(nc, ctx, "dLbc")
        dShip = Sem(nc, ctx, "dShip")
        dCtx = [Sem(nc, ctx, f"dCtx{i}") for i in range(2)]
        dSt = Sem(nc, ctx, "dSt")

        block = e(nc.Block())

        def xnc(c):
            return xn[:, c * T:(c + 1) * T]

        def rTc(c):
            return rT[:, c * T:(c + 1) * T]

        rstd_bc = bigscr[:, 0:1024]
        nmr_bc = bigscr[:, 1024:2048]
        rowA = bigscr[0:1, 0:1024]
        rowB = bigscr[0:1, 1024:2048]
        rowC = gatesT[0:1, :]

        def ccol(i):
            return consts[:, i:i + 1]

        def crow(i):
            return consts[0:1, i:i + 1]

        @block.sync
        def _(s):
            dS.inc(s.dma_start(rT[:, :], xin.ap()[:, :]), 16)
            dS.inc(s.dma_start(biasC[:, :], biasH.ap()[:, :]), 16)
            dS.inc(s.dma_start(boC[:, :], boH.ap()[:, :]), 16)
            dS.inc(s.dma_start(consts[:, :], constsH.ap()[:, :]), 16)
            dS.inc(s.dma_start(idf[:, :], identH.ap()[:, :]), 16)
            dS.inc(s.dma_start(onesf[:, :], onesH.ap()[:, :]), 16)

        @block.gpsimd
        def _(g):
            dG0.inc(g.dma_start(rwT[:, :], rw.ap()[:, :]), 16)
            dG0.inc(g.dma_start(tri[:, :], triH.ap()[:, :]), 16)
            dG0.inc(g.dma_start(idb[:, :], identH.ap()[:, :]), 16)
            dG0.inc(g.dma_start(onesb[:, :], onesH.ap()[:, :]), 16)

        @block.vector
        def _(v):
            dS.wait(v)
            i = v.tensor_scalar(bias05[:, :], biasC[:, :], ccol(C_HALF), None, op0=ALU.mult)
            sV.inc(i)

        def emit_ln(chunk_ready=None):
            baseT = sT.n
            baseS = sS.n

            @block.scalar
            def _(sc):
                if chunk_ready is None:
                    sT.wait(sc)
                    sV.wait(sc)
                    sS.wait(sc)
                    dS.wait(sc)
                for c in range(DC):
                    if chunk_ready is not None:
                        sc.wait_ge(sV.h, chunk_ready[c])
                    if c >= 1:
                        sc.wait_ge(sT.h, baseT + c)
                    i = sc.activation(sa[:, :], rTc(c), AF.Square)
                    sS.inc(i)

            @block.tensor
            def _(t):
                dS.wait(t)
                dG0.wait(t)
                if chunk_ready is None:
                    sV.wait(t)
                for c in range(DC):
                    if chunk_ready is not None:
                        t.wait_ge(sV.h, chunk_ready[c])
                    t.matmul(pp[2][0:1, 0:512], onesf[:, 0:1], rTc(c)[:, 0:512],
                             start=(c == 0), stop=(c == DC - 1))
                    t.matmul(pp[2][0:1, 512:1024], onesf[:, 0:1], rTc(c)[:, 512:1024],
                             start=(c == 0), stop=(c == DC - 1))
                    t.wait_ge(sS.h, baseS + c + 1)
                    t.matmul(pp[3][0:1, 0:512], onesb[:, 0:1], sa[:, 0:512],
                             start=(c == 0), stop=(c == DC - 1))
                    i = t.matmul(pp[3][0:1, 512:1024], onesb[:, 0:1], sa[:, 512:1024],
                                 start=(c == 0), stop=(c == DC - 1))
                    sT.inc(i)

            stats_done = sT.n

            @block.vector
            def _(v):
                v.wait_ge(sT.h, stats_done)
                v.tensor_copy(rowA[:, :], pp[2][0:1, :])
                v.tensor_copy(rowB[:, :], pp[3][0:1, :])
                v.drain()
                v.tensor_scalar(rowA[:, :], rowA[:, :], crow(C_INVD), None, op0=ALU.mult)
                v.tensor_scalar(rowB[:, :], rowB[:, :], crow(C_INVD), None, op0=ALU.mult)
                v.drain()
                v.tensor_tensor(rowC[:, :], rowA[:, :], rowA[:, :], op=ALU.mult)
                v.drain()
                v.tensor_tensor(rowB[:, :], rowB[:, :], rowC[:, :], op=ALU.subtract)
                v.drain()
                i = v.tensor_scalar(rowB[:, :], rowB[:, :], crow(C_EPS), None, op0=ALU.add)
                sV.inc(i)

            var_done = sV.n

            @block.scalar
            def _(sc):
                sc.wait_ge(sV.h, var_done)
                sc.activation(rowB[:, :], rowB[:, :], AF.Ln)
                sc.drain()
                i = sc.activation(rowB[:, :], rowB[:, :], AF.Exp, scale=-0.5)
                sS.inc(i)

            rstd_done = sS.n

            @block.vector
            def _(v):
                v.wait_ge(sS.h, rstd_done)
                i = v.scalar_tensor_tensor(rowC[:, :], rowA[:, :], crow(C_NEG1),
                                           rowB[:, :], op0=ALU.mult, op1=ALU.mult)
                sV.inc(i)

            nmr_done = sV.n

            @block.sync
            def _(s):
                s.wait_ge(sV.h, nmr_done)
                dLNb.wait(s)
                dLN.inc(s.dma_start(bcrows.ap()[0:1, :], rowB[:, :]), 16)
                dLN.inc(s.dma_start(bcrows.ap()[1:2, :], rowC[:, :]), 16)

            rows_dmad = dLN.n

            @block.gpsimd
            def _(g):
                g.wait_ge(dLN.h, rows_dmad)
                dLNb.inc(g.dma_start(rstd_bc, bass.AP(bcrows, 0, [[0, 128], [1, 1024]])), 16)
                dLNb.inc(g.dma_start(nmr_bc, bass.AP(bcrows, 1024, [[0, 128], [1, 1024]])), 16)

            bc_ready = dLNb.n

            apply_vals = []

            @block.vector
            def _(v):
                v.wait_ge(dLNb.h, bc_ready)
                for c in range(DC):
                    v.tensor_tensor(gsc[:, :], rTc(c), rstd_bc, op=ALU.mult)
                    v.drain()
                    i = v.tensor_tensor(xnc(c), gsc[:, :], nmr_bc, op=ALU.add)
                    apply_vals.append(sV.inc(i))
                    v.drain()

            return apply_vals

        def make_router(xn_apply):
            state = {"sel": [], "tr": [], "done": None}

            def emit_tt(tt):
                ps = pp[2] if tt % 2 == 0 else pp[3]

                @block.tensor
                def _(t, tt=tt, ps=ps):
                    if tt == 0:
                        t.wait_ge(sV.h, xn_apply[0])
                    if tt >= 2:
                        t.wait_ge(sV.h, state["sel"][tt - 2])
                    for c in range(DC):
                        if tt == 0 and c > 0:
                            t.wait_ge(sV.h, xn_apply[c])
                        i = t.matmul(ps[:, 0:8], xnc(c)[:, tt * 128:(tt + 1) * 128],
                                     rwT[:, c * 8:(c + 1) * 8],
                                     start=(c == 0), stop=(c == DC - 1))
                    sT.inc(i)

                lg_v = sT.n

                @block.vector
                def _(v, tt=tt, ps=ps, lg_v=lg_v):
                    v.wait_ge(sT.h, lg_v)
                    if tt >= 1:
                        v.wait_ge(sT.h, state["tr"][tt - 1])
                    v.tensor_copy(lgt[:, :], ps[:, 0:8])
                    v.drain()
                    v.reduce_max(cl1[:, :], lgt[:, :], axis=AX)
                    v.drain()
                    v.tensor_scalar(tsc[:, :], lgt[:, :], cl1[:, :], None, op0=ALU.is_ge)
                    v.drain()
                    v.scalar_tensor_tensor(tsc[:, :], tsc[:, :], ccol(C_NEGBIG), lgt[:, :],
                                           op0=ALU.mult, op1=ALU.add)
                    v.drain()
                    v.reduce_max(cl2[:, :], tsc[:, :], axis=AX)
                    v.drain()
                    v.tensor_scalar(tsc[:, :], lgt[:, :], cl2[:, :], None, op0=ALU.is_ge)
                    v.drain()
                    v.tensor_tensor(cl2[:, :], cl2[:, :], cl1[:, :], op=ALU.subtract)
                    i = v.tensor_scalar(cl4[:, :], cl1[:, :], ccol(C_NEG1), None, op0=ALU.mult)
                    sV.inc(i)

                sel_v = sV.n
                state["sel"].append(sel_v)

                @block.scalar
                def _(sc):
                    sc.wait_ge(sV.h, sel_v)
                    sc.activation(cl3[:, :], cl2[:, :], AF.Exp)
                    i = sc.activation(tsc2[:, :], lgt[:, :], AF.Exp, bias=cl4[:, :])
                    sS.inc(i)

                exp_v = sS.n

                @block.vector
                def _(v):
                    v.wait_ge(sS.h, exp_v)
                    v.tensor_scalar(cl3[:, :], cl3[:, :], ccol(C_ONE), None, op0=ALU.add)
                    v.drain()
                    v.reciprocal(cl3[:, :], cl3[:, :])
                    v.drain()
                    v.tensor_tensor(tsc[:, :], tsc[:, :], tsc2[:, :], op=ALU.mult)
                    v.drain()
                    v.tensor_scalar(tsc[:, :], tsc[:, :], cl3[:, :], None, op0=ALU.mult)
                    v.drain()
                    i = v.tensor_scalar(lgt[:, :], tsc[:, :], ccol(C_HALF), None, op0=ALU.mult)
                    sV.inc(i)

                g_v = sV.n

                @block.tensor
                def _(t, tt=tt, ps=ps):
                    t.wait_ge(sV.h, g_v)
                    i = t.transpose(ps[0:8, 512:640], lgt[:, :], idf[:, :])
                    sT.inc(i)

                tr_v = sT.n
                state["tr"].append(tr_v)

                @block.vector
                def _(v, tt=tt, ps=ps):
                    v.wait_ge(sT.h, tr_v)
                    i = v.tensor_copy(gatesT[:, tt * 128:(tt + 1) * 128], ps[0:8, 512:640])
                    sV.inc(i)

                if tt == 7:
                    state["done"] = sV.n

            return emit_tt, state

        def emit_experts(first=False, xn_apply=None, router=None):
            router_tt, router_state = router
            wis_free = [0, 0] if first else [sT.n, sT.n]
            wos_free = [0, 0] if first else [sT.n, sT.n]
            gbc_free = [0, 0] if first else [sV.n, sV.n]
            pair_a_free = [0, 0] if first else [sS.n, sS.n]
            pair_b_free = [0, 0] if first else [sV.n, sV.n]
            o_free = [0, 0] if first else [sV.n, sV.n]
            sa_free = [0] if first else [sV.n]
            prev_stage2_mm = [0] if first else [sT.n]
            dt_final = []

            def emit_gate_bc(ex):
                par = ex % 2

                @block.sync
                def _(s, ex=ex, par=par):
                    s.wait_ge(sV.h, router_state["done"])
                    dGbc[par].wait(s)
                    i = s.dma_start(bcrows.ap()[2 + par:3 + par, :], gatesT[ex:ex + 1, :])
                    dRowG[par].inc(i, 16)

                grow_v = dRowG[par].n

                @block.gpsimd
                def _(g, ex=ex, par=par, grow_v=grow_v):
                    g.wait_ge(dRowG[par].h, grow_v)
                    g.wait_ge(sV.h, gbc_free[par])
                    i = g.dma_start(gbc[:, par * 1024:(par + 1) * 1024],
                                    bass.AP(bcrows, (2 + par) * 1024, [[0, 128], [1, 1024]]))
                    dGbc[par].inc(i, 16)

                return dGbc[par].n

            for ex in range(NE):
                if ex > 0:
                    gbc_ready = emit_gate_bc(ex)

                for sl in range(NSL):
                    slot = sl % 2
                    pslot = 0 if ex == 0 else slot
                    pa = pp[0] if pslot == 0 else pp[2]
                    pb = pp[1] if pslot == 0 else pp[3]

                    if not (first and ex == 0 and sl < 2):
                        @block.gpsimd
                        def _(g, ex=ex, sl=sl, slot=slot):
                            g.wait_ge(sT.h, wis_free[slot])
                            i = g.dma_start(wis[:, slot * 4096:(slot + 1) * 4096],
                                            win.ap()[ex, sl])
                            dWin[slot].inc(i, 16)

                    slab_ld = dWin[slot].n

                    @block.tensor
                    def _(t, ex=ex, sl=sl, slot=slot, pslot=pslot, pa=pa, pb=pb,
                          slab_ld=slab_ld):
                        t.wait_ge(dWin[slot].h, slab_ld)
                        if ex == 1 and sl <= 1:
                            t.wait_ge(sV.h, router_state["done"])
                        if ex == 0 and sl == 2:
                            t.wait_ge(sV.h, xn_apply[-1])
                        if sl == 0:
                            t.wait_ge(sV.h, max(o_free))
                        t.wait_ge(sS.h, pair_a_free[pslot])
                        t.wait_ge(sV.h, pair_b_free[pslot])
                        for half, ps in ((0, pa), (1, pb)):
                            for c in range(DC):
                                if ex == 0 and sl <= 1:
                                    t.wait_ge(sV.h, xn_apply[c])
                                lhs = wis[:, slot * 4096 + c * 256 + half * 128:
                                          slot * 4096 + c * 256 + half * 128 + 128]
                                for nh in range(2):
                                    i = t.matmul(ps[:, nh * 512:(nh + 1) * 512], lhs,
                                                 xnc(c)[:, nh * 512:(nh + 1) * 512],
                                                 start=(c == 0), stop=(c == DC - 1))
                            sT.inc(i)
                        wis_free[slot] = sT.n

                    a_done = sT.n - 1
                    b_done = sT.n

                    @block.scalar
                    def _(sc, pslot=pslot, a_done=a_done, pa=pa):
                        sc.wait_ge(sT.h, a_done)
                        sc.wait_ge(sV.h, sa_free[0])
                        i = sc.activation(sa[:, :], pa[:, :], AF.Silu)
                        sS.inc(i)

                    pair_a_free[pslot] = sS.n
                    silu_v = sS.n

                    @block.vector
                    def _(v, ex=ex, sl=sl, b_done=b_done, silu_v=silu_v, pb=pb):
                        v.wait_ge(sT.h, b_done)
                        v.wait_ge(sS.h, silu_v)
                        if sl == 0:
                            v.wait_ge(sT.h, prev_stage2_mm[0])
                        i = v.tensor_tensor(gT[:, sl * T:(sl + 1) * T], sa[:, :], pb[:, :],
                                            op=ALU.mult)
                        sV.inc(i)

                    pair_b_free[pslot] = sV.n
                    sa_free[0] = sV.n

                    if ex == 0 and sl < 8:
                        router_tt(sl)

                if ex == 0:
                    gbc_ready = emit_gate_bc(0)

                g_all_done = sV.n

                for dt in range(NDT):
                    po = pp[0] if dt % 2 == 0 else pp[1]
                    for jh in range(2):
                        slot = jh

                        if not (first and ex == 0 and dt == 0):
                            @block.gpsimd
                            def _(g, ex=ex, dt=dt, jh=jh, slot=slot):
                                g.wait_ge(sT.h, wos_free[slot])
                                i = g.dma_start(wos[:, slot * 2048:(slot + 1) * 2048],
                                                wout.ap()[ex, dt, jh])
                                dWout[slot].inc(i, 16)

                        oslab_ld = dWout[slot].n

                        @block.tensor
                        def _(t, dt=dt, jh=jh, slot=slot, po=po, oslab_ld=oslab_ld):
                            t.wait_ge(dWout[slot].h, oslab_ld)
                            if jh == 0:
                                t.wait_ge(sV.h, g_all_done)
                                t.wait_ge(sV.h, o_free[dt % 2])
                            for jc16 in range(16):
                                jc = jh * 16 + jc16
                                lhs = wos[:, slot * 2048 + jc16 * 128:slot * 2048 + (jc16 + 1) * 128]
                                for nh in range(2):
                                    i = t.matmul(po[:, nh * 512:(nh + 1) * 512], lhs,
                                                 gT[:, jc * T + nh * 512:jc * T + (nh + 1) * 512],
                                                 start=(jh == 0 and jc16 == 0),
                                                 stop=(jh == 1 and jc16 == 15))
                            sT.inc(i)
                            wos_free[slot] = sT.n

                    o_done = sT.n

                    @block.vector
                    def _(v, ex=ex, dt=dt, po=po, o_done=o_done, gbc_ready=gbc_ready):
                        v.wait_ge(sT.h, o_done)
                        v.wait_ge(dGbc[ex % 2].h, gbc_ready)
                        v.tensor_tensor(gsc[:, :], po[:, :],
                                        gbc[:, (ex % 2) * 1024:(ex % 2 + 1) * 1024], op=ALU.mult)
                        v.drain()
                        i = v.tensor_tensor(rTc(dt), rTc(dt), gsc[:, :], op=ALU.add)
                        if ex == NE - 1:
                            v.drain()
                            i = v.tensor_scalar(rTc(dt), rTc(dt), bias05[:, dt:dt + 1],
                                                None, op0=ALU.add)
                        sV.inc(i)
                        v.drain()
                        o_free[dt % 2] = sV.n
                        if ex == NE - 1:
                            dt_final.append(sV.n)

                gbc_free[ex % 2] = sV.n
                prev_stage2_mm[0] = sT.n

            return dt_final
        def emit_attn(xn_apply=None):
            xnB = gT[:, 0:DC * T]
            pTs = gT[:, 16384:24576]
            qbuf = gT[:, 24576:26624]
            kbuf = gT[:, 26624:28672]
            vbuf = gT[:, 30720:32768]

            def pT(kt):
                return pTs[:, kt * 512:(kt + 1) * 512]

            attn_start_T = sT.n

            @block.sync
            def _(s):
                if xn_apply is not None:
                    s.wait_ge(sV.h, xn_apply[7])
                else:
                    sV.wait(s)
                s.wait_ge(sT.h, attn_start_T)
                i = s.dma_start(cc_xn_in.ap()[0:1024, :].rearrange("(c p) t -> p c t", p=128),
                                xn[:, 0:8 * T].rearrange("p (c t) -> p c t", t=T))
                dXn.inc(i, 16)

            @block.sync
            def _(s):
                if xn_apply is not None:
                    s.wait_ge(sV.h, xn_apply[15])
                else:
                    sV.wait(s)
                i = s.dma_start(cc_xn_in.ap()[1024:2048, :].rearrange("(c p) t -> p c t", p=128),
                                xn[:, 8 * T:16 * T].rearrange("p (c t) -> p c t", t=T))
                dXnB.inc(i, 16)

            shipA, shipB = dXn.n, dXnB.n

            @block.gpsimd
            def _(g):
                g.wait_ge(dXn.h, shipA)
                i = g.collective_compute(
                    "AllGather", ALU.bypass,
                    replica_groups=[[0, 1], [2, 3], [4, 5], [6, 7]],
                    ins=[cc_xn_in.ap()[0:1024, :].opt()],
                    outs=[cc_xn_outA.ap().opt()],
                )
                sCC.inc(i)

            xng_A = sCC.n

            @block.gpsimd
            def _(g):
                g.wait_ge(dXnB.h, shipB)
                i = g.collective_compute(
                    "AllGather", ALU.bypass,
                    replica_groups=[[0, 1], [2, 3], [4, 5], [6, 7]],
                    ins=[cc_xn_in.ap()[1024:2048, :].opt()],
                    outs=[cc_xn_outB.ap().opt()],
                )
                sCC.inc(i)

            xng_B = sCC.n

            @block.sync
            def _(s):
                s.wait_ge(sCC.h, xng_A)
                dXn.inc(s.dma_start(
                    xn[:, 0:8 * T].rearrange("p (c t) -> p c t", t=T),
                    cc_xn_outA.ap()[0:1024, :].rearrange("(c p) t -> p c t", p=128)), 16)
                dXn.inc(s.dma_start(
                    xnB[:, 0:8 * T].rearrange("p (c t) -> p c t", t=T),
                    cc_xn_outA.ap()[1024:2048, :].rearrange("(c p) t -> p c t", p=128)), 16)

            @block.sync
            def _(s):
                s.wait_ge(sCC.h, xng_B)
                dXnB.inc(s.dma_start(
                    xn[:, 8 * T:16 * T].rearrange("p (c t) -> p c t", t=T),
                    cc_xn_outB.ap()[0:1024, :].rearrange("(c p) t -> p c t", p=128)), 16)
                dXnB.inc(s.dma_start(
                    xnB[:, 8 * T:16 * T].rearrange("p (c t) -> p c t", t=T),
                    cc_xn_outB.ap()[1024:2048, :].rearrange("(c p) t -> p c t", p=128)), 16)

            halves_A, halves_B = dXn.n, dXnB.n

            def seqc(c, col0, w=512):
                if col0 < T:
                    return xnc(c)[:, col0:col0 + w]
                return xnB[:, c * T + col0 - T:c * T + col0 - T + w]

            def wq_sl(slot):
                return wis[:, 0:2048] if slot == 0 else wis[:, 6144:8192]

            def wk_sl(slot):
                return wis[:, 2048:4096] if slot == 0 else wos[:, 0:2048]

            def wv_sl(slot):
                return wis[:, 4096:6144] if slot == 0 else wos[:, 2048:4096]

            aw_free = [sT.n, sT.n]
            pT_free = [sT.n] * 16
            sslot_exp = [sS.n] * 4
            l_read = [sV.n]
            ctx_evac = [sV.n, sV.n]
            vslot_ev = [sV.n, sV.n]
            prev_ship = [dShip.n]
            proj_ev = []

            def emit_awload(hh):
                slot = hh % 2

                @block.gpsimd
                def _(g, hh=hh, slot=slot):
                    g.wait_ge(sT.h, aw_free[slot])
                    dAw[slot].inc(g.dma_start(wq_sl(slot), wq.ap()[hh]), 16)
                    dAw[slot].inc(g.dma_start(wk_sl(slot), wk.ap()[hh]), 16)
                    dAw[slot].inc(g.dma_start(wv_sl(slot), wv.ap()[hh]), 16)
                return dAw[slot].n

            awl_vals = [emit_awload(0), emit_awload(1)]

            for h in range(NH):
                slot = h % 2
                awl = awl_vals[h]
                head_prev_exp = sS.n

                for wbuf, dbuf in ((wq_sl(slot), qbuf), (wk_sl(slot), kbuf)):
                    for qs in range(4):
                        psl = pp[qs % 2]
                        tile = psl[:, 0:512] if qs < 2 else psl[:, 512:1024]

                        @block.tensor
                        def _(t, wbuf=wbuf, qs=qs, tile=tile, awl=awl,
                              head_prev_exp=head_prev_exp, slot=slot):
                            t.wait_ge(dAw[slot].h, awl)
                            t.wait_ge(dXn.h, halves_A)
                            t.wait_ge(dXnB.h, halves_B)
                            t.wait_ge(sS.h, head_prev_exp)
                            if len(proj_ev) >= 4:
                                t.wait_ge(sV.h, proj_ev[-4])
                            for c in range(DC):
                                i = t.matmul(tile,
                                             wbuf[:, c * 128:(c + 1) * 128],
                                             seqc(c, qs * 512),
                                             start=(c == 0), stop=(c == DC - 1))
                            sT.inc(i)

                        mm_v = sT.n

                        @block.vector
                        def _(v, qs=qs, dbuf=dbuf, tile=tile, mm_v=mm_v):
                            v.wait_ge(sT.h, mm_v)
                            i = v.tensor_copy(dbuf[:, qs * 512:(qs + 1) * 512], tile)
                            proj_ev.append(sV.inc(i))

                vp_vals = []

                def emit_vevac(kt):
                    vps = pp[3][:, 512:640] if kt % 2 == 0 else pp[3][:, 640:768]
                    wait_v = vp_vals[min(kt + 1, 15)]

                    @block.vector
                    def _(v, kt=kt, vps=vps, wait_v=wait_v):
                        v.wait_ge(sT.h, wait_v)
                        i = v.tensor_copy(vbuf[:, kt * 128:(kt + 1) * 128], vps)
                        sV.inc(i)

                    vslot_ev[kt % 2] = sV.n

                for kt in range(16):
                    vps = pp[3][:, 512:640] if kt % 2 == 0 else pp[3][:, 640:768]

                    @block.tensor
                    def _(t, kt=kt, awl=awl, vps=vps, slot=slot):
                        t.wait_ge(dAw[slot].h, awl)
                        if kt == 0:
                            t.wait_ge(dXn.h, halves_A)
                            t.wait_ge(dXnB.h, halves_B)
                        if kt >= 2:
                            t.wait_ge(sV.h, vslot_ev[kt % 2])
                        for c in range(DC):
                            i = t.matmul(vps, seqc(c, kt * 128, 128),
                                         wv_sl(slot)[:, c * 128:(c + 1) * 128],
                                         start=(c == 0), stop=(c == DC - 1))
                        sT.inc(i)

                    vp_vals.append(sT.n)
                    if kt >= 1:
                        emit_vevac(kt - 1)
                emit_vevac(15)

                aw_free[slot] = sT.n
                vtr_done = sV.n
                if h + 2 < NH:
                    awl_vals.append(emit_awload(h + 2))

                for qs in range(4):
                    nkt = 4 * qs + 4
                    lslot = pp[3][0:1, 0:512]
                    cslot = pp[2][:, 0:512] if qs % 2 == 0 else pp[2][:, 512:1024]
                    lrow = bigscr[0:1, 0:512] if qs % 2 == 0 else bigscr[0:1, 512:1024]
                    lbcb = bigscr[:, 1024:1536] if qs % 2 == 0 else bigscr[:, 1536:2048]
                    brow = 4 + qs % 2
                    s_done = {}
                    mask_done = {}

                    def emit_s(kt):
                        sps = [pp[0][:, 0:512], pp[0][:, 512:1024],
                               pp[1][:, 0:512], pp[1][:, 512:1024]][kt % 4]

                        @block.tensor
                        def _(t, qs=qs, kt=kt, sps=sps):
                            if kt == 0 and qs == 0:
                                t.wait_ge(sV.h, vtr_done)
                            t.wait_ge(sS.h, sslot_exp[kt % 4])
                            i = t.matmul(sps, kbuf[:, kt * 128:(kt + 1) * 128],
                                         qbuf[:, qs * 512:(qs + 1) * 512],
                                         start=True, stop=True)
                            sT.inc(i)

                        smm_v = sT.n

                        @block.scalar
                        def _(sc, kt=kt, sps=sps, smm_v=smm_v):
                            sc.wait_ge(sT.h, smm_v)
                            sc.wait_ge(sT.h, pT_free[kt])
                            i = sc.activation(pT(kt), sps, AF.Exp, bias=ccol(C_NEGSHIFT),
                                              scale=float(1.0 / LAM))
                            sS.inc(i)

                        s_done[kt] = sS.n
                        sslot_exp[kt % 4] = sS.n

                        if kt >= 4 * qs:
                            off = (kt - 4 * qs) * 128

                            @block.vector
                            def _(v, kt=kt, off=off):
                                v.wait_ge(sS.h, s_done[kt])
                                if off > 0:
                                    v.tensor_scalar(pT(kt)[:, 0:off], pT(kt)[:, 0:off],
                                                    ccol(C_ZERO), None, op0=ALU.mult)
                                i = v.tensor_tensor(pT(kt)[:, off:off + 128],
                                                    pT(kt)[:, off:off + 128], tri[:, :],
                                                    op=ALU.mult)
                                mask_done[kt] = sV.inc(i)
                                v.drain()
                        else:
                            mask_done[kt] = None

                    for kt in range(nkt):
                        emit_s(kt)

                    @block.tensor
                    def _(t, qs=qs, nkt=nkt, lslot=lslot):
                        t.wait_ge(sV.h, l_read[0])
                        for kt in range(nkt):
                            if mask_done[kt] is not None:
                                t.wait_ge(sV.h, mask_done[kt])
                            else:
                                t.wait_ge(sS.h, s_done[kt])
                            i = t.matmul(lslot, onesb[:, 0:1], pT(kt),
                                         start=(kt == 0), stop=(kt == nkt - 1))
                        sT.inc(i)

                    l_done = sT.n

                    @block.tensor
                    def _(t, qs=qs, nkt=nkt, cslot=cslot):
                        t.wait_ge(sV.h, ctx_evac[qs % 2])
                        for kt in range(nkt):
                            i = t.matmul(cslot, vbuf[:, kt * 128:(kt + 1) * 128],
                                         pT(kt), start=(kt == 0), stop=(kt == nkt - 1))
                            sT.inc(i)
                            pT_free[kt] = sT.n

                    lav_v = sT.n

                    @block.vector
                    def _(v, l_done=l_done, lslot=lslot, lrow=lrow):
                        v.wait_ge(sT.h, l_done)
                        i = v.reciprocal(lrow, lslot)
                        sV.inc(i)

                    l_read[0] = sV.n
                    lrec_v = sV.n

                    @block.sync
                    def _(s, lrec_v=lrec_v, lrow=lrow, brow=brow, qs=qs):
                        s.wait_ge(sV.h, lrec_v)
                        i = s.dma_start(bcrows.ap()[brow:brow + 1, 0:512], lrow)
                        dLRow.inc(i, 16)

                    lrow_v = dLRow.n

                    @block.gpsimd
                    def _(g, lrow_v=lrow_v, lbcb=lbcb, brow=brow):
                        g.wait_ge(dLRow.h, lrow_v)
                        i = g.dma_start(lbcb, bass.AP(bcrows, brow * 1024, [[0, 128], [1, 512]]))
                        dLbc.inc(i, 16)

                    lbc_v = dLbc.n

                    @block.vector
                    def _(v, lbc_v=lbc_v, cslot=cslot, lbcb=lbcb, lav_v=lav_v):
                        v.wait_ge(dLbc.h, lbc_v)
                        v.wait_ge(sT.h, lav_v)
                        v.wait_ge(dShip.h, prev_ship[0])
                        i = v.tensor_tensor(gsc[:, 0:512], cslot, lbcb, op=ALU.mult)
                        sV.inc(i)
                        v.drain()

                    ctx_evac[qs % 2] = sV.n
                    ce_v = sV.n

                    @block.sync
                    def _(s, h=h, qs=qs, ce_v=ce_v):
                        s.wait_ge(sV.h, ce_v)
                        i = s.dma_start(
                            cc_ctx_in.ap()[h * 128:(h + 1) * 128, qs * 512:(qs + 1) * 512],
                            gsc[:, 0:512])
                        dShip.inc(i, 16)

                    prev_ship[0] = dShip.n
                if h == 3:
                    shipA = dShip.n

                    @block.gpsimd
                    def _(g, shipA=shipA):
                        g.wait_ge(dShip.h, shipA)
                        i = g.collective_compute(
                            "AllGather", ALU.bypass,
                            replica_groups=[[0, 1], [2, 3], [4, 5], [6, 7]],
                            ins=[cc_ctx_in.ap()[0:512, :].opt()],
                            outs=[cc_ctx_outA.ap().opt()],
                        )
                        sCC.inc(i)
                    ctx_gatheredA = sCC.n

            ctx_shipped = dShip.n

            @block.gpsimd
            def _(g):
                g.wait_ge(dShip.h, ctx_shipped)
                i = g.collective_compute(
                    "AllGather", ALU.bypass,
                    replica_groups=[[0, 1], [2, 3], [4, 5], [6, 7]],
                    ins=[cc_ctx_in.ap()[512:1024, :].opt()],
                    outs=[cc_ctx_outB.ap().opt()],
                )
                sCC.inc(i)

            ctx_gatheredB = sCC.n

            def ctx_src(ic):
                if ic < 4:
                    return cc_ctx_outA, ic * 128, ctx_gatheredA
                if ic < 8:
                    return cc_ctx_outB, (ic - 4) * 128, ctx_gatheredB
                if ic < 12:
                    return cc_ctx_outA, 512 + (ic - 8) * 128, ctx_gatheredA
                return cc_ctx_outB, 512 + (ic - 12) * 128, ctx_gatheredB

            ic_order = [0, 1, 2, 3, 8, 9, 10, 11, 4, 5, 6, 7, 12, 13, 14, 15]
            ctxbuf = gT[:, 0:DC * T]
            comb_done = {}
            comb_seq = []
            for pos, ic in enumerate(ic_order):
                pl = pos % 2
                tileL = pTs[:, pl * 2048:pl * 2048 + 1024]
                tileR = pTs[:, pl * 2048 + 1024:pl * 2048 + 2048]
                buf, row, gath_v = ctx_src(ic)

                @block.sync
                def _(s, pos=pos, tileL=tileL, tileR=tileR, buf=buf, row=row, gath_v=gath_v):
                    s.wait_ge(sCC.h, gath_v)
                    if pos >= 2:
                        s.wait_ge(sV.h, comb_done[ic_order[pos - 2]])
                    dCtx[pos % 2].inc(s.dma_start(tileL, buf.ap()[row:row + 128, 0:1024]), 16)
                    dCtx[pos % 2].inc(s.dma_start(tileR, buf.ap()[row:row + 128, 1024:2048]), 16)

                ld_v = dCtx[pos % 2].n

                @block.vector
                def _(v, ic=ic, pos=pos, ld_v=ld_v, tileL=tileL, tileR=tileR):
                    v.wait_ge(dCtx[pos % 2].h, ld_v)
                    v.tensor_scalar(sa[:, :], tileL, ccol(C_S0), None, op0=ALU.mult)
                    v.drain()
                    i = v.scalar_tensor_tensor(ctxbuf[:, ic * T:(ic + 1) * T], tileR,
                                               ccol(C_S1), sa[:, :], op0=ALU.mult, op1=ALU.add)
                    comb_done[ic] = sV.inc(i)
                    comb_seq.append(comb_done[ic])
                    v.drain()

            wo_free = [sT.n, sT.n]
            oo_free = [sV.n, sV.n]
            attn_dt_final = []
            for i2t in range(NDT):
                slot = i2t % 2

                @block.gpsimd
                def _(g, i2t=i2t, slot=slot):
                    g.wait_ge(sT.h, wo_free[slot])
                    i = g.dma_start(wos[:, slot * 2048:(slot + 1) * 2048], wo.ap()[i2t])
                    dWo[slot].inc(i, 16)

                wol_v = dWo[slot].n

                @block.tensor
                def _(t, i2t=i2t, slot=slot, wol_v=wol_v):
                    t.wait_ge(dWo[slot].h, wol_v)
                    po = pp[0] if i2t % 2 == 0 else pp[1]
                    t.wait_ge(sV.h, oo_free[i2t % 2])
                    for k, ic in enumerate(ic_order):
                        t.wait_ge(sV.h, comb_done[ic])
                        lhs = wos[:, slot * 2048 + ic * 128:slot * 2048 + (ic + 1) * 128]
                        for nh in range(2):
                            i = t.matmul(po[:, nh * 512:(nh + 1) * 512], lhs,
                                         ctxbuf[:, ic * T + nh * 512:ic * T + (nh + 1) * 512],
                                         start=(k == 0), stop=(k == DC - 1))
                    sT.inc(i)
                    wo_free[slot] = sT.n

                omm_v = sT.n

                @block.vector
                def _(v, i2t=i2t, omm_v=omm_v):
                    v.wait_ge(sT.h, omm_v)
                    if i2t == 0:
                        v.wait_ge(dShip.h, ctx_shipped)
                    po = pp[0] if i2t % 2 == 0 else pp[1]
                    v.tensor_scalar(gsc[:, :], po[:, :], boC[:, i2t:i2t + 1], None, op0=ALU.add)
                    v.drain()
                    i = v.tensor_tensor(rTc(i2t), rTc(i2t), gsc[:, :], op=ALU.add)
                    sV.inc(i)
                    v.drain()
                    oo_free[i2t % 2] = sV.n
                    attn_dt_final.append(sV.n)

            return attn_dt_final

        if "ffn1" in stages:
            @block.gpsimd
            def _(g):
                dWin[0].inc(g.dma_start(wis[:, 0:4096], win.ap()[0, 0]), 16)
                dWin[1].inc(g.dma_start(wis[:, 4096:8192], win.ap()[0, 1]), 16)
                dWout[0].inc(g.dma_start(wos[:, 0:2048], wout.ap()[0, 0, 0]), 16)
                dWout[1].inc(g.dma_start(wos[:, 2048:4096], wout.ap()[0, 0, 1]), 16)

            ln1 = emit_ln()
            ffn1_dt = emit_experts(first=True, xn_apply=ln1, router=make_router(ln1))
        if "attn" in stages:
            ln2 = emit_ln(chunk_ready=ffn1_dt if "ffn1" in stages else None)
            attn_dt = emit_attn(xn_apply=ln2)
        last_bias = None
        if "ffn2" in stages:
            ln3 = emit_ln(chunk_ready=attn_dt if "attn" in stages else None)
            last_bias = emit_experts(xn_apply=ln3, router=make_router(ln3))

        @block.sync
        def _(s):
            sT.wait(s)
            sS.wait(s)
            if last_bias is not None:
                for dt in range(NDT):
                    s.wait_ge(sV.h, last_bias[dt])
                    dSt.inc(s.dma_start(out.ap()[:, dt * T:(dt + 1) * T], rTc(dt)), 16)
            else:
                sV.wait(s)
                dSt.inc(s.dma_start(out.ap()[:, :], rT[:, :]), 16)
            s.wait_ge(dSt.h, dSt.n)

    return nc



def _prep_inputs(L0_attn_inputs, L0_attn_w_q, L0_attn_w_k, L0_attn_w_v, L0_attn_w_o,
                 L0_attn_b_o, L0_ffn_router_weight, L0_ffn_w_in, L0_ffn_w_out, L0_ffn_bias):
    f = np.float32
    x = np.ascontiguousarray(np.asarray(L0_attn_inputs, f)).reshape(8 * T, D)
    w_q = np.asarray(L0_attn_w_q, f); w_k = np.asarray(L0_attn_w_k, f)
    w_v = np.asarray(L0_attn_w_v, f); w_o = np.asarray(L0_attn_w_o, f)
    b_o = np.asarray(L0_attn_b_o, f)
    rwt = np.asarray(L0_ffn_router_weight, f)
    w_in = np.asarray(L0_ffn_w_in, f); w_out_ = np.asarray(L0_ffn_w_out, f)
    bias = np.asarray(L0_ffn_bias, f)

    perm = np.concatenate([np.concatenate([np.arange(jc * 128, (jc + 1) * 128),
                                           np.arange(4096 + jc * 128, 4096 + (jc + 1) * 128)])
                           for jc in range(32)])
    win_h = np.ascontiguousarray(
        w_in[:, perm, :].reshape(NE, NSL, 256, DC, 128).transpose(0, 1, 4, 3, 2)
        .reshape(NE, NSL, 128, 4096))
    woutT = w_out_.transpose(0, 2, 1)
    wout_h = np.ascontiguousarray(
        woutT.reshape(NE, 2, 16, 128, NDT, 128).transpose(0, 4, 1, 3, 2, 5)
        .reshape(NE, NDT, 2, 128, 2048))

    def byhead(w):
        return np.ascontiguousarray(
            w.T.reshape(DC, 128, 16, 128).transpose(2, 1, 0, 3).reshape(16, 128, 2048))

    wq_all = byhead(w_q); wk_all = byhead(w_k); wv_all = byhead(w_v)
    wo_h = np.ascontiguousarray(
        w_o.T.reshape(DC, 128, NDT, 128).transpose(2, 1, 0, 3).reshape(NDT, 128, 2048))
    rw_h = np.ascontiguousarray(rwt.T.reshape(DC, 128, 8).transpose(1, 0, 2).reshape(128, DC * 8))
    bias_h = np.ascontiguousarray(bias.reshape(16, 128).T)
    bo_h = np.ascontiguousarray(b_o.reshape(16, 128).T)
    tri_h = np.ascontiguousarray(np.triu(np.ones((128, 128), f)))
    id_h = np.ascontiguousarray(np.eye(128, dtype=f))
    ones_h = np.zeros((128, 8), f); ones_h[:, 0] = 1.0

    in_maps = []
    for c in range(8):
        xc = np.ascontiguousarray(
            x[c * T:(c + 1) * T, :].T.reshape(DC, 128, T).transpose(1, 0, 2).reshape(128, DC * T))
        consts = np.zeros((128, NCONST), f)
        consts[:, C_HALF] = 0.5
        consts[:, C_ONE] = 1.0
        consts[:, C_EPS] = 1e-5
        consts[:, C_INVD] = 1.0 / D
        consts[:, C_NEG1] = -1.0
        consts[:, C_NEGBIG] = -1e9
        consts[:, C_NEGSHIFT] = -SHIFT
        consts[:, C_S0] = 1.0 if c % 2 == 0 else 0.0
        consts[:, C_S1] = 0.0 if c % 2 == 0 else 1.0
        hsel = slice(0, 8) if c % 2 == 0 else slice(8, 16)
        in_maps.append({
            "xin": xc,
            "win": win_h,
            "wout": wout_h,
            "wq": np.ascontiguousarray(wq_all[hsel]),
            "wk": np.ascontiguousarray(wk_all[hsel]),
            "wv": np.ascontiguousarray(wv_all[hsel]),
            "wo": wo_h,
            "rw": rw_h,
            "biasH": bias_h,
            "boH": bo_h,
            "constsH": consts,
            "triH": tri_h,
            "identH": id_h,
            "onesH": ones_h,
        })
    return in_maps


def kernel(**inputs):
    in_maps = _prep_inputs(**inputs)
    nc = build()
    res = run_bass_kernel_spmd(nc, in_maps, core_ids=list(range(8)))
    parts = []
    for c in range(8):
        arr = np.asarray(res.results[c]["out"]).reshape(128, DC, T)
        parts.append(arr.transpose(1, 0, 2).reshape(D, T).T)
    full = np.concatenate(parts, axis=0).reshape(4, 2048, D)
    return np.ascontiguousarray(full.astype(np.float32))


# version 07 (final)
# speedup vs baseline: 1.0322x; 1.0322x over previous
import numpy as np
from contextlib import ExitStack

import concourse.bass as bass
import concourse.mybir as mybir
from concourse.bass_utils import run_bass_kernel_spmd

F32 = mybir.dt.float32
BF16 = mybir.dt.bfloat16
AX = mybir.AxisListType.X
AF = mybir.ActivationFunctionType
ALU = mybir.AluOpType

D = 2048
DC = 16
T = 1024
NE = 8
NSL = 32
NDT = 16
NH = 8
S = 2048
LAM = float(np.float32(2048.0) ** 0.5)

C_HALF, C_ONE, C_EPS, C_INVD, C_NEG1, C_NEGBIG, C_NEGSHIFT, C_S0, C_S1, C_ZERO = range(10)
NCONST = 16
SHIFT = 4.0

STAGES = ("ffn1", "attn", "ffn2")


class Sem:
    def __init__(self, nc, ctx, name):
        self.h = ctx.enter_context(nc.semaphore(name))
        self.n = 0

    def inc(self, instr, k=1):
        instr.then_inc(self.h, k)
        self.n += k
        assert self.n < 32000, f"semaphore overflow: {self.n}"
        return self.n

    def wait(self, eng, val=None):
        v = self.n if val is None else val
        if v > 0:
            eng.wait_ge(self.h, v)


def build(stages=STAGES):
    nc = bass.Bass("TRN2", target_bir_lowering=False)

    xin = nc.dram_tensor("xin", [128, DC * T], F32, kind="ExternalInput")
    win = nc.dram_tensor("win", [NE, NSL, 128, 4096], F32, kind="ExternalInput")
    wout = nc.dram_tensor("wout", [NE, NDT, 2, 128, 2048], F32, kind="ExternalInput")
    wq = nc.dram_tensor("wq", [NH, 128, 2048], F32, kind="ExternalInput")
    wk = nc.dram_tensor("wk", [NH, 128, 2048], F32, kind="ExternalInput")
    wv = nc.dram_tensor("wv", [NH, 128, 2048], F32, kind="ExternalInput")
    wo = nc.dram_tensor("wo", [NDT, 128, 2048], F32, kind="ExternalInput")
    rw = nc.dram_tensor("rw", [128, DC * 8], F32, kind="ExternalInput")
    biasH = nc.dram_tensor("biasH", [128, 16], F32, kind="ExternalInput")
    boH = nc.dram_tensor("boH", [128, 16], F32, kind="ExternalInput")
    constsH = nc.dram_tensor("constsH", [128, NCONST], F32, kind="ExternalInput")
    triH = nc.dram_tensor("triH", [128, 128], F32, kind="ExternalInput")
    identH = nc.dram_tensor("identH", [128, 128], F32, kind="ExternalInput")
    onesH = nc.dram_tensor("onesH", [128, 8], F32, kind="ExternalInput")
    out = nc.dram_tensor("out", [128, DC * T], F32, kind="ExternalOutput")

    cc_xn_in = nc.dram_tensor("cc_xn_in", [S, T], BF16, kind="Internal")
    cc_xn_outA = nc.dram_tensor("cc_xn_outA", [S, T], BF16, kind="Internal")
    cc_xn_outB = nc.dram_tensor("cc_xn_outB", [S, T], BF16, kind="Internal")
    cc_ctx_in = nc.dram_tensor("cc_ctx_in", [1024, S], BF16, kind="Internal")
    cc_ctx_outA = nc.dram_tensor("cc_ctx_outA", [1024, S], BF16, kind="Internal")
    cc_ctx_outB = nc.dram_tensor("cc_ctx_outB", [1024, S], BF16, kind="Internal")
    bcrows = nc.dram_tensor("bcrows", [8, 1024], F32, kind="Internal")

    with ExitStack() as ctx:
        e = ctx.enter_context
        rT = e(nc.sbuf_tensor("rT", [128, DC * T], F32))
        xn = e(nc.sbuf_tensor("xn", [128, DC * T], BF16))
        gT = e(nc.sbuf_tensor("gT", [128, 32 * T], BF16))
        wis = e(nc.sbuf_tensor("wis", [128, 2 * 4096], BF16))
        wos = e(nc.sbuf_tensor("wos", [128, 2 * 2048], BF16))
        sa = e(nc.sbuf_tensor("sa", [128, 1024], BF16))
        gsc = e(nc.sbuf_tensor("gsc", [128, 1024], BF16))
        gbc = e(nc.sbuf_tensor("gbc", [128, 2 * 1024], BF16))
        bigscr = e(nc.sbuf_tensor("bigscr", [128, 2048], F32))
        gatesT = e(nc.sbuf_tensor("gatesT", [8, 1024], F32))
        biasC = e(nc.sbuf_tensor("biasC", [128, 16], F32))
        bias05 = e(nc.sbuf_tensor("bias05", [128, 16], F32))
        boC = e(nc.sbuf_tensor("boC", [128, 16], F32))
        consts = e(nc.sbuf_tensor("consts", [128, NCONST], F32))
        rwT = e(nc.sbuf_tensor("rwT", [128, 128], BF16))
        tri = e(nc.sbuf_tensor("tri", [128, 128], BF16))
        idf = e(nc.sbuf_tensor("idf", [128, 128], F32))
        idb = e(nc.sbuf_tensor("idb", [128, 128], BF16))
        onesb = e(nc.sbuf_tensor("onesb", [128, 8], BF16))
        onesf = e(nc.sbuf_tensor("onesf", [128, 8], F32))
        lgt = e(nc.sbuf_tensor("lgt", [128, 8], F32))
        tsc = e(nc.sbuf_tensor("tsc", [128, 8], F32))
        tsc2 = e(nc.sbuf_tensor("tsc2", [128, 8], F32))
        cl1 = e(nc.sbuf_tensor("cl1", [128, 1], F32))
        cl2 = e(nc.sbuf_tensor("cl2", [128, 1], F32))
        cl3 = e(nc.sbuf_tensor("cl3", [128, 1], F32))
        cl4 = e(nc.sbuf_tensor("cl4", [128, 1], F32))

        pp = [e(nc.psum_tensor(f"pp{i}", [128, 1024], F32)) for i in range(4)]

        sT = Sem(nc, ctx, "sT")
        sV = Sem(nc, ctx, "sV")
        sS = Sem(nc, ctx, "sS")
        sCC = Sem(nc, ctx, "sCC")
        dS = Sem(nc, ctx, "dS")
        dX0 = [Sem(nc, ctx, f"dX0{i}") for i in range(4)]
        dG0 = Sem(nc, ctx, "dG0")
        dLN = Sem(nc, ctx, "dLN")
        dLNb = Sem(nc, ctx, "dLNb")
        dRowG = [Sem(nc, ctx, f"dRowG{i}") for i in range(2)]
        dGbc = [Sem(nc, ctx, f"dGbc{i}") for i in range(2)]
        dWin = [Sem(nc, ctx, f"dWin{i}") for i in range(2)]
        dWout = [Sem(nc, ctx, f"dWout{i}") for i in range(2)]
        dWo = [Sem(nc, ctx, f"dWo{i}") for i in range(2)]
        dAw = [Sem(nc, ctx, f"dAw{i}") for i in range(2)]
        dXn = Sem(nc, ctx, "dXn")
        dXnB = Sem(nc, ctx, "dXnB")
        dLRow = Sem(nc, ctx, "dLRow")
        dLbc = Sem(nc, ctx, "dLbc")
        dShip = Sem(nc, ctx, "dShip")
        dCtx = [Sem(nc, ctx, f"dCtx{i}") for i in range(2)]
        dSt = Sem(nc, ctx, "dSt")

        block = e(nc.Block())

        def xnc(c):
            return xn[:, c * T:(c + 1) * T]

        def rTc(c):
            return rT[:, c * T:(c + 1) * T]

        rstd_bc = bigscr[:, 0:1024]
        nmr_bc = bigscr[:, 1024:2048]
        rowA = bigscr[0:1, 0:1024]
        rowB = bigscr[0:1, 1024:2048]
        rowC = gatesT[0:1, :]

        def ccol(i):
            return consts[:, i:i + 1]

        def crow(i):
            return consts[0:1, i:i + 1]

        xgrp = []
        small_ld = []

        @block.sync
        def _(s):
            dS.inc(s.dma_start(biasC[:, :], biasH.ap()[:, :]), 16)
            dS.inc(s.dma_start(boC[:, :], boH.ap()[:, :]), 16)
            dS.inc(s.dma_start(consts[:, :], constsH.ap()[:, :]), 16)
            dS.inc(s.dma_start(idf[:, :], identH.ap()[:, :]), 16)
            dS.inc(s.dma_start(onesf[:, :], onesH.ap()[:, :]), 16)
            small_ld.append(dS.n)
            for g4 in range(4):
                dX0[g4].inc(s.dma_start(rT[:, g4 * 4 * T:(g4 + 1) * 4 * T],
                                        xin.ap()[:, g4 * 4 * T:(g4 + 1) * 4 * T]), 16)
                xgrp.append((g4, dX0[g4].n))

        @block.gpsimd
        def _(g):
            dG0.inc(g.dma_start(rwT[:, :], rw.ap()[:, :]), 16)
            dG0.inc(g.dma_start(tri[:, :], triH.ap()[:, :]), 16)
            dG0.inc(g.dma_start(idb[:, :], identH.ap()[:, :]), 16)
            dG0.inc(g.dma_start(onesb[:, :], onesH.ap()[:, :]), 16)

        @block.vector
        def _(v):
            dS.wait(v)
            i = v.tensor_scalar(bias05[:, :], biasC[:, :], ccol(C_HALF), None, op0=ALU.mult)
            sV.inc(i)

        def emit_ln(chunk_ready=None):
            baseT = sT.n
            baseS = sS.n

            first_ln = (sT.n == 0)

            @block.scalar
            def _(sc):
                if chunk_ready is None and not first_ln:
                    sT.wait(sc)
                    sV.wait(sc)
                    sS.wait(sc)
                    dS.wait(sc)
                for c in range(DC):
                    if chunk_ready is not None:
                        sc.wait_ge(sV.h, chunk_ready[c])
                    elif first_ln:
                        sc.wait_ge(dX0[xgrp[c // 4][0]].h, xgrp[c // 4][1])
                    if c >= 1:
                        sc.wait_ge(sT.h, baseT + c)
                    i = sc.activation(sa[:, :], rTc(c), AF.Square)
                    sS.inc(i)

            @block.tensor
            def _(t):
                dG0.wait(t)
                if chunk_ready is None and not first_ln:
                    dS.wait(t)
                    sV.wait(t)
                elif first_ln:
                    t.wait_ge(dS.h, small_ld[0])
                for c in range(DC):
                    if chunk_ready is not None:
                        t.wait_ge(sV.h, chunk_ready[c])
                    elif first_ln:
                        t.wait_ge(dX0[xgrp[c // 4][0]].h, xgrp[c // 4][1])
                    t.matmul(pp[2][0:1, 0:512], onesf[:, 0:1], rTc(c)[:, 0:512],
                             start=(c == 0), stop=(c == DC - 1))
                    t.matmul(pp[2][0:1, 512:1024], onesf[:, 0:1], rTc(c)[:, 512:1024],
                             start=(c == 0), stop=(c == DC - 1))
                    t.wait_ge(sS.h, baseS + c + 1)
                    t.matmul(pp[3][0:1, 0:512], onesb[:, 0:1], sa[:, 0:512],
                             start=(c == 0), stop=(c == DC - 1))
                    i = t.matmul(pp[3][0:1, 512:1024], onesb[:, 0:1], sa[:, 512:1024],
                                 start=(c == 0), stop=(c == DC - 1))
                    sT.inc(i)

            stats_done = sT.n

            @block.vector
            def _(v):
                v.wait_ge(sT.h, stats_done)
                v.tensor_copy(rowA[:, :], pp[2][0:1, :])
                v.tensor_copy(rowB[:, :], pp[3][0:1, :])
                v.drain()
                v.tensor_scalar(rowA[:, :], rowA[:, :], crow(C_INVD), None, op0=ALU.mult)
                v.tensor_scalar(rowB[:, :], rowB[:, :], crow(C_INVD), None, op0=ALU.mult)
                v.drain()
                v.tensor_tensor(rowC[:, :], rowA[:, :], rowA[:, :], op=ALU.mult)
                v.drain()
                v.tensor_tensor(rowB[:, :], rowB[:, :], rowC[:, :], op=ALU.subtract)
                v.drain()
                i = v.tensor_scalar(rowB[:, :], rowB[:, :], crow(C_EPS), None, op0=ALU.add)
                sV.inc(i)

            var_done = sV.n

            @block.scalar
            def _(sc):
                sc.wait_ge(sV.h, var_done)
                sc.activation(rowB[:, :], rowB[:, :], AF.Ln)
                sc.drain()
                i = sc.activation(rowB[:, :], rowB[:, :], AF.Exp, scale=-0.5)
                sS.inc(i)

            rstd_done = sS.n

            @block.vector
            def _(v):
                v.wait_ge(sS.h, rstd_done)
                i = v.scalar_tensor_tensor(rowC[:, :], rowA[:, :], crow(C_NEG1),
                                           rowB[:, :], op0=ALU.mult, op1=ALU.mult)
                sV.inc(i)

            nmr_done = sV.n

            @block.sync
            def _(s):
                s.wait_ge(sV.h, nmr_done)
                dLNb.wait(s)
                dLN.inc(s.dma_start(bcrows.ap()[0:1, :], rowB[:, :]), 16)
                dLN.inc(s.dma_start(bcrows.ap()[1:2, :], rowC[:, :]), 16)

            rows_dmad = dLN.n

            @block.gpsimd
            def _(g):
                g.wait_ge(dLN.h, rows_dmad)
                dLNb.inc(g.dma_start(rstd_bc, bass.AP(bcrows, 0, [[0, 128], [1, 1024]])), 16)
                dLNb.inc(g.dma_start(nmr_bc, bass.AP(bcrows, 1024, [[0, 128], [1, 1024]])), 16)

            bc_ready = dLNb.n

            apply_vals = []

            @block.vector
            def _(v):
                v.wait_ge(dLNb.h, bc_ready)
                for c in range(DC):
                    v.tensor_tensor(gsc[:, :], rTc(c), rstd_bc, op=ALU.mult)
                    v.drain()
                    i = v.tensor_tensor(xnc(c), gsc[:, :], nmr_bc, op=ALU.add)
                    apply_vals.append(sV.inc(i))
                    v.drain()

            return apply_vals

        def make_router(xn_apply):
            state = {"sel": [], "tr": [], "done": None}

            def emit_tt(tt):
                ps = pp[2] if tt % 2 == 0 else pp[3]

                @block.tensor
                def _(t, tt=tt, ps=ps):
                    if tt == 0:
                        t.wait_ge(sV.h, xn_apply[0])
                    if tt >= 2:
                        t.wait_ge(sV.h, state["sel"][tt - 2])
                    for c in range(DC):
                        if tt == 0 and c > 0:
                            t.wait_ge(sV.h, xn_apply[c])
                        i = t.matmul(ps[:, 0:8], xnc(c)[:, tt * 128:(tt + 1) * 128],
                                     rwT[:, c * 8:(c + 1) * 8],
                                     start=(c == 0), stop=(c == DC - 1))
                    sT.inc(i)

                lg_v = sT.n

                @block.vector
                def _(v, tt=tt, ps=ps, lg_v=lg_v):
                    v.wait_ge(sT.h, lg_v)
                    if tt >= 1:
                        v.wait_ge(sT.h, state["tr"][tt - 1])
                    v.tensor_copy(lgt[:, :], ps[:, 0:8])
                    v.drain()
                    v.reduce_max(cl1[:, :], lgt[:, :], axis=AX)
                    v.drain()
                    v.tensor_scalar(tsc[:, :], lgt[:, :], cl1[:, :], None, op0=ALU.is_ge)
                    v.drain()
                    v.scalar_tensor_tensor(tsc[:, :], tsc[:, :], ccol(C_NEGBIG), lgt[:, :],
                                           op0=ALU.mult, op1=ALU.add)
                    v.drain()
                    v.reduce_max(cl2[:, :], tsc[:, :], axis=AX)
                    v.drain()
                    v.tensor_scalar(tsc[:, :], lgt[:, :], cl2[:, :], None, op0=ALU.is_ge)
                    v.drain()
                    v.tensor_tensor(cl2[:, :], cl2[:, :], cl1[:, :], op=ALU.subtract)
                    i = v.tensor_scalar(cl4[:, :], cl1[:, :], ccol(C_NEG1), None, op0=ALU.mult)
                    sV.inc(i)

                sel_v = sV.n
                state["sel"].append(sel_v)

                @block.scalar
                def _(sc):
                    sc.wait_ge(sV.h, sel_v)
                    sc.activation(cl3[:, :], cl2[:, :], AF.Exp)
                    i = sc.activation(tsc2[:, :], lgt[:, :], AF.Exp, bias=cl4[:, :])
                    sS.inc(i)

                exp_v = sS.n

                @block.vector
                def _(v):
                    v.wait_ge(sS.h, exp_v)
                    v.tensor_scalar(cl3[:, :], cl3[:, :], ccol(C_ONE), None, op0=ALU.add)
                    v.drain()
                    v.reciprocal(cl3[:, :], cl3[:, :])
                    v.drain()
                    v.tensor_tensor(tsc[:, :], tsc[:, :], tsc2[:, :], op=ALU.mult)
                    v.drain()
                    v.tensor_scalar(tsc[:, :], tsc[:, :], cl3[:, :], None, op0=ALU.mult)
                    v.drain()
                    i = v.tensor_scalar(lgt[:, :], tsc[:, :], ccol(C_HALF), None, op0=ALU.mult)
                    sV.inc(i)

                g_v = sV.n

                @block.tensor
                def _(t, tt=tt, ps=ps):
                    t.wait_ge(sV.h, g_v)
                    i = t.transpose(ps[0:8, 512:640], lgt[:, :], idf[:, :])
                    sT.inc(i)

                tr_v = sT.n
                state["tr"].append(tr_v)

                @block.vector
                def _(v, tt=tt, ps=ps):
                    v.wait_ge(sT.h, tr_v)
                    i = v.tensor_copy(gatesT[:, tt * 128:(tt + 1) * 128], ps[0:8, 512:640])
                    sV.inc(i)

                if tt == 7:
                    state["done"] = sV.n

            return emit_tt, state

        def emit_experts(first=False, xn_apply=None, router=None):
            router_tt, router_state = router
            wis_free = [0, 0] if first else [sT.n, sT.n]
            wos_free = [0, 0] if first else [sT.n, sT.n]
            gbc_free = [0, 0] if first else [sV.n, sV.n]
            pair_a_free = [0, 0] if first else [sS.n, sS.n]
            pair_b_free = [0, 0] if first else [sV.n, sV.n]
            o_free = [0, 0] if first else [sV.n, sV.n]
            sa_free = [0] if first else [sV.n]
            prev_stage2_mm = [0] if first else [sT.n]
            dt_final = []

            def emit_gate_bc(ex):
                par = ex % 2

                @block.sync
                def _(s, ex=ex, par=par):
                    s.wait_ge(sV.h, router_state["done"])
                    dGbc[par].wait(s)
                    i = s.dma_start(bcrows.ap()[2 + par:3 + par, :], gatesT[ex:ex + 1, :])
                    dRowG[par].inc(i, 16)

                grow_v = dRowG[par].n

                @block.gpsimd
                def _(g, ex=ex, par=par, grow_v=grow_v):
                    g.wait_ge(dRowG[par].h, grow_v)
                    g.wait_ge(sV.h, gbc_free[par])
                    i = g.dma_start(gbc[:, par * 1024:(par + 1) * 1024],
                                    bass.AP(bcrows, (2 + par) * 1024, [[0, 128], [1, 1024]]))
                    dGbc[par].inc(i, 16)

                return dGbc[par].n

            for ex in range(NE):
                if ex > 0:
                    gbc_ready = emit_gate_bc(ex)

                for sl in range(NSL):
                    slot = sl % 2
                    pslot = 0 if (ex == 0 and sl < 9) else slot
                    pa = pp[0] if pslot == 0 else pp[2]
                    pb = pp[1] if pslot == 0 else pp[3]

                    if not (first and ex == 0 and sl < 2):
                        @block.gpsimd
                        def _(g, ex=ex, sl=sl, slot=slot):
                            g.wait_ge(sT.h, wis_free[slot])
                            i = g.dma_start(wis[:, slot * 4096:(slot + 1) * 4096],
                                            win.ap()[ex, sl])
                            dWin[slot].inc(i, 16)

                    slab_ld = dWin[slot].n

                    @block.tensor
                    def _(t, ex=ex, sl=sl, slot=slot, pslot=pslot, pa=pa, pb=pb,
                          slab_ld=slab_ld):
                        t.wait_ge(dWin[slot].h, slab_ld)
                        if ex == 0 and sl == 9:
                            t.wait_ge(sV.h, router_state["done"])
                        if ex == 0 and sl == 2:
                            t.wait_ge(sV.h, xn_apply[-1])
                        if sl == 0:
                            t.wait_ge(sV.h, max(o_free))
                        t.wait_ge(sS.h, pair_a_free[pslot])
                        t.wait_ge(sV.h, pair_b_free[pslot])
                        for half, ps in ((0, pa), (1, pb)):
                            for c in range(DC):
                                if ex == 0 and sl <= 1:
                                    t.wait_ge(sV.h, xn_apply[c])
                                lhs = wis[:, slot * 4096 + c * 256 + half * 128:
                                          slot * 4096 + c * 256 + half * 128 + 128]
                                for nh in range(2):
                                    i = t.matmul(ps[:, nh * 512:(nh + 1) * 512], lhs,
                                                 xnc(c)[:, nh * 512:(nh + 1) * 512],
                                                 start=(c == 0), stop=(c == DC - 1))
                            sT.inc(i)
                        wis_free[slot] = sT.n

                    a_done = sT.n - 1
                    b_done = sT.n

                    @block.scalar
                    def _(sc, pslot=pslot, a_done=a_done, pa=pa):
                        sc.wait_ge(sT.h, a_done)
                        sc.wait_ge(sV.h, sa_free[0])
                        i = sc.activation(sa[:, :], pa[:, :], AF.Silu)
                        sS.inc(i)

                    pair_a_free[pslot] = sS.n
                    silu_v = sS.n

                    @block.vector
                    def _(v, ex=ex, sl=sl, b_done=b_done, silu_v=silu_v, pb=pb):
                        v.wait_ge(sT.h, b_done)
                        v.wait_ge(sS.h, silu_v)
                        if sl == 0:
                            v.wait_ge(sT.h, prev_stage2_mm[0])
                        i = v.tensor_tensor(gT[:, sl * T:(sl + 1) * T], sa[:, :], pb[:, :],
                                            op=ALU.mult)
                        sV.inc(i)

                    pair_b_free[pslot] = sV.n
                    sa_free[0] = sV.n

                    if ex == 0 and sl < 8:
                        router_tt(sl)

                if ex == 0:
                    gbc_ready = emit_gate_bc(0)

                g_all_done = sV.n

                for dt in range(NDT):
                    po = pp[0] if dt % 2 == 0 else pp[1]
                    for jh in range(2):
                        slot = jh

                        if not (first and ex == 0 and dt == 0):
                            @block.gpsimd
                            def _(g, ex=ex, dt=dt, jh=jh, slot=slot):
                                g.wait_ge(sT.h, wos_free[slot])
                                i = g.dma_start(wos[:, slot * 2048:(slot + 1) * 2048],
                                                wout.ap()[ex, dt, jh])
                                dWout[slot].inc(i, 16)

                        oslab_ld = dWout[slot].n

                        @block.tensor
                        def _(t, dt=dt, jh=jh, slot=slot, po=po, oslab_ld=oslab_ld):
                            t.wait_ge(dWout[slot].h, oslab_ld)
                            if jh == 0:
                                t.wait_ge(sV.h, g_all_done)
                                t.wait_ge(sV.h, o_free[dt % 2])
                            for jc16 in range(16):
                                jc = jh * 16 + jc16
                                lhs = wos[:, slot * 2048 + jc16 * 128:slot * 2048 + (jc16 + 1) * 128]
                                for nh in range(2):
                                    i = t.matmul(po[:, nh * 512:(nh + 1) * 512], lhs,
                                                 gT[:, jc * T + nh * 512:jc * T + (nh + 1) * 512],
                                                 start=(jh == 0 and jc16 == 0),
                                                 stop=(jh == 1 and jc16 == 15))
                            sT.inc(i)
                            wos_free[slot] = sT.n

                    o_done = sT.n

                    @block.vector
                    def _(v, ex=ex, dt=dt, po=po, o_done=o_done, gbc_ready=gbc_ready):
                        v.wait_ge(sT.h, o_done)
                        v.wait_ge(dGbc[ex % 2].h, gbc_ready)
                        v.tensor_tensor(gsc[:, :], po[:, :],
                                        gbc[:, (ex % 2) * 1024:(ex % 2 + 1) * 1024], op=ALU.mult)
                        v.drain()
                        i = v.tensor_tensor(rTc(dt), rTc(dt), gsc[:, :], op=ALU.add)
                        if ex == NE - 1:
                            v.drain()
                            i = v.tensor_scalar(rTc(dt), rTc(dt), bias05[:, dt:dt + 1],
                                                None, op0=ALU.add)
                        sV.inc(i)
                        v.drain()
                        o_free[dt % 2] = sV.n
                        if ex == NE - 1:
                            dt_final.append(sV.n)

                gbc_free[ex % 2] = sV.n
                prev_stage2_mm[0] = sT.n

            return dt_final
        def emit_attn(xn_apply=None):
            xnB = gT[:, 0:DC * T]
            pTs = gT[:, 16384:24576]
            qbuf = gT[:, 24576:26624]
            kbuf = gT[:, 26624:28672]
            vbuf = gT[:, 30720:32768]

            def pT(kt):
                return pTs[:, kt * 512:(kt + 1) * 512]

            attn_start_T = sT.n

            @block.sync
            def _(s):
                if xn_apply is not None:
                    s.wait_ge(sV.h, xn_apply[7])
                else:
                    sV.wait(s)
                s.wait_ge(sT.h, attn_start_T)
                i = s.dma_start(cc_xn_in.ap()[0:1024, :].rearrange("(c p) t -> p c t", p=128),
                                xn[:, 0:8 * T].rearrange("p (c t) -> p c t", t=T))
                dXn.inc(i, 16)

            @block.sync
            def _(s):
                if xn_apply is not None:
                    s.wait_ge(sV.h, xn_apply[15])
                else:
                    sV.wait(s)
                i = s.dma_start(cc_xn_in.ap()[1024:2048, :].rearrange("(c p) t -> p c t", p=128),
                                xn[:, 8 * T:16 * T].rearrange("p (c t) -> p c t", t=T))
                dXnB.inc(i, 16)

            shipA, shipB = dXn.n, dXnB.n

            @block.gpsimd
            def _(g):
                g.wait_ge(dXn.h, shipA)
                i = g.collective_compute(
                    "AllGather", ALU.bypass,
                    replica_groups=[[0, 1], [2, 3], [4, 5], [6, 7]],
                    ins=[cc_xn_in.ap()[0:1024, :].opt()],
                    outs=[cc_xn_outA.ap().opt()],
                )
                sCC.inc(i)

            xng_A = sCC.n

            @block.gpsimd
            def _(g):
                g.wait_ge(dXnB.h, shipB)
                i = g.collective_compute(
                    "AllGather", ALU.bypass,
                    replica_groups=[[0, 1], [2, 3], [4, 5], [6, 7]],
                    ins=[cc_xn_in.ap()[1024:2048, :].opt()],
                    outs=[cc_xn_outB.ap().opt()],
                )
                sCC.inc(i)

            xng_B = sCC.n

            @block.sync
            def _(s):
                s.wait_ge(sCC.h, xng_A)
                dXn.inc(s.dma_start(
                    xn[:, 0:8 * T].rearrange("p (c t) -> p c t", t=T),
                    cc_xn_outA.ap()[0:1024, :].rearrange("(c p) t -> p c t", p=128)), 16)
                dXn.inc(s.dma_start(
                    xnB[:, 0:8 * T].rearrange("p (c t) -> p c t", t=T),
                    cc_xn_outA.ap()[1024:2048, :].rearrange("(c p) t -> p c t", p=128)), 16)

            @block.sync
            def _(s):
                s.wait_ge(sCC.h, xng_B)
                dXnB.inc(s.dma_start(
                    xn[:, 8 * T:16 * T].rearrange("p (c t) -> p c t", t=T),
                    cc_xn_outB.ap()[0:1024, :].rearrange("(c p) t -> p c t", p=128)), 16)
                dXnB.inc(s.dma_start(
                    xnB[:, 8 * T:16 * T].rearrange("p (c t) -> p c t", t=T),
                    cc_xn_outB.ap()[1024:2048, :].rearrange("(c p) t -> p c t", p=128)), 16)

            halves_A, halves_B = dXn.n, dXnB.n

            def seqc(c, col0, w=512):
                if col0 < T:
                    return xnc(c)[:, col0:col0 + w]
                return xnB[:, c * T + col0 - T:c * T + col0 - T + w]

            def wq_sl(slot):
                return wis[:, 0:2048] if slot == 0 else wis[:, 6144:8192]

            def wk_sl(slot):
                return wis[:, 2048:4096] if slot == 0 else wos[:, 0:2048]

            def wv_sl(slot):
                return wis[:, 4096:6144] if slot == 0 else wos[:, 2048:4096]

            aw_free = [sT.n, sT.n]
            pT_free = [sT.n] * 16
            sslot_exp = [sS.n] * 4
            l_read = [sV.n]
            ctx_evac = [sV.n, sV.n]
            vslot_ev = [sV.n, sV.n]
            prev_ship = [dShip.n]
            proj_ev = []

            def emit_awload(hh):
                slot = hh % 2

                @block.gpsimd
                def _(g, hh=hh, slot=slot):
                    g.wait_ge(sT.h, aw_free[slot])
                    dAw[slot].inc(g.dma_start(wq_sl(slot), wq.ap()[hh]), 16)
                    dAw[slot].inc(g.dma_start(wk_sl(slot), wk.ap()[hh]), 16)
                    dAw[slot].inc(g.dma_start(wv_sl(slot), wv.ap()[hh]), 16)
                return dAw[slot].n

            awl_vals = [emit_awload(0), emit_awload(1)]

            for h in range(NH):
                slot = h % 2
                awl = awl_vals[h]
                head_prev_exp = sS.n

                for wbuf, dbuf in ((wq_sl(slot), qbuf), (wk_sl(slot), kbuf)):
                    for qs in range(4):
                        psl = pp[qs % 2]
                        tile = psl[:, 0:512] if qs < 2 else psl[:, 512:1024]

                        @block.tensor
                        def _(t, wbuf=wbuf, qs=qs, tile=tile, awl=awl,
                              head_prev_exp=head_prev_exp, slot=slot):
                            t.wait_ge(dAw[slot].h, awl)
                            t.wait_ge(dXn.h, halves_A)
                            t.wait_ge(dXnB.h, halves_B)
                            t.wait_ge(sS.h, head_prev_exp)
                            if len(proj_ev) >= 4:
                                t.wait_ge(sV.h, proj_ev[-4])
                            for c in range(DC):
                                i = t.matmul(tile,
                                             wbuf[:, c * 128:(c + 1) * 128],
                                             seqc(c, qs * 512),
                                             start=(c == 0), stop=(c == DC - 1))
                            sT.inc(i)

                        mm_v = sT.n

                        @block.vector
                        def _(v, qs=qs, dbuf=dbuf, tile=tile, mm_v=mm_v):
                            v.wait_ge(sT.h, mm_v)
                            i = v.tensor_copy(dbuf[:, qs * 512:(qs + 1) * 512], tile)
                            proj_ev.append(sV.inc(i))

                vp_vals = []

                def emit_vevac(kt):
                    vps = pp[3][:, 512:640] if kt % 2 == 0 else pp[3][:, 640:768]
                    wait_v = vp_vals[min(kt + 1, 15)]

                    @block.vector
                    def _(v, kt=kt, vps=vps, wait_v=wait_v):
                        v.wait_ge(sT.h, wait_v)
                        i = v.tensor_copy(vbuf[:, kt * 128:(kt + 1) * 128], vps)
                        sV.inc(i)

                    vslot_ev[kt % 2] = sV.n

                for kt in range(16):
                    vps = pp[3][:, 512:640] if kt % 2 == 0 else pp[3][:, 640:768]

                    @block.tensor
                    def _(t, kt=kt, awl=awl, vps=vps, slot=slot):
                        t.wait_ge(dAw[slot].h, awl)
                        if kt == 0:
                            t.wait_ge(dXn.h, halves_A)
                            t.wait_ge(dXnB.h, halves_B)
                        if kt >= 2:
                            t.wait_ge(sV.h, vslot_ev[kt % 2])
                        for c in range(DC):
                            i = t.matmul(vps, seqc(c, kt * 128, 128),
                                         wv_sl(slot)[:, c * 128:(c + 1) * 128],
                                         start=(c == 0), stop=(c == DC - 1))
                        sT.inc(i)

                    vp_vals.append(sT.n)
                    if kt >= 1:
                        emit_vevac(kt - 1)
                emit_vevac(15)

                aw_free[slot] = sT.n
                vtr_done = sV.n
                if h + 2 < NH:
                    awl_vals.append(emit_awload(h + 2))

                for qs in range(4):
                    nkt = 4 * qs + 4
                    lslot = pp[3][0:1, 0:512]
                    cslot = pp[2][:, 0:512] if qs % 2 == 0 else pp[2][:, 512:1024]
                    lrow = bigscr[0:1, 0:512] if qs % 2 == 0 else bigscr[0:1, 512:1024]
                    lbcb = bigscr[:, 1024:1536] if qs % 2 == 0 else bigscr[:, 1536:2048]
                    brow = 4 + qs % 2
                    s_done = {}
                    mask_done = {}

                    def emit_s(kt):
                        sps = [pp[0][:, 0:512], pp[0][:, 512:1024],
                               pp[1][:, 0:512], pp[1][:, 512:1024]][kt % 4]

                        @block.tensor
                        def _(t, qs=qs, kt=kt, sps=sps):
                            if kt == 0 and qs == 0:
                                t.wait_ge(sV.h, vtr_done)
                            t.wait_ge(sS.h, sslot_exp[kt % 4])
                            i = t.matmul(sps, kbuf[:, kt * 128:(kt + 1) * 128],
                                         qbuf[:, qs * 512:(qs + 1) * 512],
                                         start=True, stop=True)
                            sT.inc(i)

                        smm_v = sT.n

                        @block.scalar
                        def _(sc, kt=kt, sps=sps, smm_v=smm_v):
                            sc.wait_ge(sT.h, smm_v)
                            sc.wait_ge(sT.h, pT_free[kt])
                            i = sc.activation(pT(kt), sps, AF.Exp, bias=ccol(C_NEGSHIFT),
                                              scale=float(1.0 / LAM))
                            sS.inc(i)

                        s_done[kt] = sS.n
                        sslot_exp[kt % 4] = sS.n

                        if kt >= 4 * qs:
                            off = (kt - 4 * qs) * 128

                            @block.vector
                            def _(v, kt=kt, off=off):
                                v.wait_ge(sS.h, s_done[kt])
                                if off > 0:
                                    v.tensor_scalar(pT(kt)[:, 0:off], pT(kt)[:, 0:off],
                                                    ccol(C_ZERO), None, op0=ALU.mult)
                                i = v.tensor_tensor(pT(kt)[:, off:off + 128],
                                                    pT(kt)[:, off:off + 128], tri[:, :],
                                                    op=ALU.mult)
                                mask_done[kt] = sV.inc(i)
                                v.drain()
                        else:
                            mask_done[kt] = None

                    for kt in range(nkt):
                        emit_s(kt)

                    @block.tensor
                    def _(t, qs=qs, nkt=nkt, lslot=lslot):
                        t.wait_ge(sV.h, l_read[0])
                        for kt in range(nkt):
                            if mask_done[kt] is not None:
                                t.wait_ge(sV.h, mask_done[kt])
                            else:
                                t.wait_ge(sS.h, s_done[kt])
                            i = t.matmul(lslot, onesb[:, 0:1], pT(kt),
                                         start=(kt == 0), stop=(kt == nkt - 1))
                        sT.inc(i)

                    l_done = sT.n

                    @block.tensor
                    def _(t, qs=qs, nkt=nkt, cslot=cslot):
                        t.wait_ge(sV.h, ctx_evac[qs % 2])
                        for kt in range(nkt):
                            i = t.matmul(cslot, vbuf[:, kt * 128:(kt + 1) * 128],
                                         pT(kt), start=(kt == 0), stop=(kt == nkt - 1))
                            sT.inc(i)
                            pT_free[kt] = sT.n

                    lav_v = sT.n

                    @block.vector
                    def _(v, l_done=l_done, lslot=lslot, lrow=lrow):
                        v.wait_ge(sT.h, l_done)
                        i = v.reciprocal(lrow, lslot)
                        sV.inc(i)

                    l_read[0] = sV.n
                    lrec_v = sV.n

                    @block.sync
                    def _(s, lrec_v=lrec_v, lrow=lrow, brow=brow, qs=qs):
                        s.wait_ge(sV.h, lrec_v)
                        i = s.dma_start(bcrows.ap()[brow:brow + 1, 0:512], lrow)
                        dLRow.inc(i, 16)

                    lrow_v = dLRow.n

                    @block.gpsimd
                    def _(g, lrow_v=lrow_v, lbcb=lbcb, brow=brow):
                        g.wait_ge(dLRow.h, lrow_v)
                        i = g.dma_start(lbcb, bass.AP(bcrows, brow * 1024, [[0, 128], [1, 512]]))
                        dLbc.inc(i, 16)

                    lbc_v = dLbc.n

                    @block.vector
                    def _(v, lbc_v=lbc_v, cslot=cslot, lbcb=lbcb, lav_v=lav_v):
                        v.wait_ge(dLbc.h, lbc_v)
                        v.wait_ge(sT.h, lav_v)
                        v.wait_ge(dShip.h, prev_ship[0])
                        i = v.tensor_tensor(gsc[:, 0:512], cslot, lbcb, op=ALU.mult)
                        sV.inc(i)
                        v.drain()

                    ctx_evac[qs % 2] = sV.n
                    ce_v = sV.n

                    @block.sync
                    def _(s, h=h, qs=qs, ce_v=ce_v):
                        s.wait_ge(sV.h, ce_v)
                        i = s.dma_start(
                            cc_ctx_in.ap()[h * 128:(h + 1) * 128, qs * 512:(qs + 1) * 512],
                            gsc[:, 0:512])
                        dShip.inc(i, 16)

                    prev_ship[0] = dShip.n
                if h == 3:
                    shipA = dShip.n

                    @block.gpsimd
                    def _(g, shipA=shipA):
                        g.wait_ge(dShip.h, shipA)
                        i = g.collective_compute(
                            "AllGather", ALU.bypass,
                            replica_groups=[[0, 1], [2, 3], [4, 5], [6, 7]],
                            ins=[cc_ctx_in.ap()[0:512, :].opt()],
                            outs=[cc_ctx_outA.ap().opt()],
                        )
                        sCC.inc(i)
                    ctx_gatheredA = sCC.n

            ctx_shipped = dShip.n

            @block.gpsimd
            def _(g):
                g.wait_ge(dShip.h, ctx_shipped)
                i = g.collective_compute(
                    "AllGather", ALU.bypass,
                    replica_groups=[[0, 1], [2, 3], [4, 5], [6, 7]],
                    ins=[cc_ctx_in.ap()[512:1024, :].opt()],
                    outs=[cc_ctx_outB.ap().opt()],
                )
                sCC.inc(i)

            ctx_gatheredB = sCC.n

            def ctx_src(ic):
                if ic < 4:
                    return cc_ctx_outA, ic * 128, ctx_gatheredA
                if ic < 8:
                    return cc_ctx_outB, (ic - 4) * 128, ctx_gatheredB
                if ic < 12:
                    return cc_ctx_outA, 512 + (ic - 8) * 128, ctx_gatheredA
                return cc_ctx_outB, 512 + (ic - 12) * 128, ctx_gatheredB

            ic_order = [0, 1, 2, 3, 8, 9, 10, 11, 4, 5, 6, 7, 12, 13, 14, 15]
            ctxbuf = gT[:, 0:DC * T]
            comb_done = {}
            comb_seq = []
            for pos, ic in enumerate(ic_order):
                pl = pos % 2
                tileL = pTs[:, pl * 2048:pl * 2048 + 1024]
                tileR = pTs[:, pl * 2048 + 1024:pl * 2048 + 2048]
                buf, row, gath_v = ctx_src(ic)

                @block.sync
                def _(s, pos=pos, tileL=tileL, tileR=tileR, buf=buf, row=row, gath_v=gath_v):
                    s.wait_ge(sCC.h, gath_v)
                    if pos >= 2:
                        s.wait_ge(sV.h, comb_done[ic_order[pos - 2]])
                    dCtx[pos % 2].inc(s.dma_start(tileL, buf.ap()[row:row + 128, 0:1024]), 16)
                    dCtx[pos % 2].inc(s.dma_start(tileR, buf.ap()[row:row + 128, 1024:2048]), 16)

                ld_v = dCtx[pos % 2].n

                @block.vector
                def _(v, ic=ic, pos=pos, ld_v=ld_v, tileL=tileL, tileR=tileR):
                    v.wait_ge(dCtx[pos % 2].h, ld_v)
                    v.tensor_scalar(sa[:, :], tileL, ccol(C_S0), None, op0=ALU.mult)
                    v.drain()
                    i = v.scalar_tensor_tensor(ctxbuf[:, ic * T:(ic + 1) * T], tileR,
                                               ccol(C_S1), sa[:, :], op0=ALU.mult, op1=ALU.add)
                    comb_done[ic] = sV.inc(i)
                    comb_seq.append(comb_done[ic])
                    v.drain()

            wo_free = [sT.n, sT.n]
            oo_free = [sV.n, sV.n]
            attn_dt_final = []
            for i2t in range(NDT):
                slot = i2t % 2

                @block.gpsimd
                def _(g, i2t=i2t, slot=slot):
                    g.wait_ge(sT.h, wo_free[slot])
                    i = g.dma_start(wos[:, slot * 2048:(slot + 1) * 2048], wo.ap()[i2t])
                    dWo[slot].inc(i, 16)

                wol_v = dWo[slot].n

                @block.tensor
                def _(t, i2t=i2t, slot=slot, wol_v=wol_v):
                    t.wait_ge(dWo[slot].h, wol_v)
                    po = pp[0] if i2t % 2 == 0 else pp[1]
                    t.wait_ge(sV.h, oo_free[i2t % 2])
                    for k, ic in enumerate(ic_order):
                        t.wait_ge(sV.h, comb_done[ic])
                        lhs = wos[:, slot * 2048 + ic * 128:slot * 2048 + (ic + 1) * 128]
                        for nh in range(2):
                            i = t.matmul(po[:, nh * 512:(nh + 1) * 512], lhs,
                                         ctxbuf[:, ic * T + nh * 512:ic * T + (nh + 1) * 512],
                                         start=(k == 0), stop=(k == DC - 1))
                    sT.inc(i)
                    wo_free[slot] = sT.n

                omm_v = sT.n

                @block.vector
                def _(v, i2t=i2t, omm_v=omm_v):
                    v.wait_ge(sT.h, omm_v)
                    if i2t == 0:
                        v.wait_ge(dShip.h, ctx_shipped)
                    po = pp[0] if i2t % 2 == 0 else pp[1]
                    v.tensor_scalar(gsc[:, :], po[:, :], boC[:, i2t:i2t + 1], None, op0=ALU.add)
                    v.drain()
                    i = v.tensor_tensor(rTc(i2t), rTc(i2t), gsc[:, :], op=ALU.add)
                    sV.inc(i)
                    v.drain()
                    oo_free[i2t % 2] = sV.n
                    attn_dt_final.append(sV.n)

            return attn_dt_final

        if "ffn1" in stages:
            @block.gpsimd
            def _(g):
                dWin[0].inc(g.dma_start(wis[:, 0:4096], win.ap()[0, 0]), 16)
                dWin[1].inc(g.dma_start(wis[:, 4096:8192], win.ap()[0, 1]), 16)
                dWout[0].inc(g.dma_start(wos[:, 0:2048], wout.ap()[0, 0, 0]), 16)
                dWout[1].inc(g.dma_start(wos[:, 2048:4096], wout.ap()[0, 0, 1]), 16)

            ln1 = emit_ln()
            ffn1_dt = emit_experts(first=True, xn_apply=ln1, router=make_router(ln1))
        if "attn" in stages:
            ln2 = emit_ln(chunk_ready=ffn1_dt if "ffn1" in stages else None)
            attn_dt = emit_attn(xn_apply=ln2)
        last_bias = None
        if "ffn2" in stages:
            ln3 = emit_ln(chunk_ready=attn_dt if "attn" in stages else None)
            last_bias = emit_experts(xn_apply=ln3, router=make_router(ln3))

        @block.sync
        def _(s):
            sT.wait(s)
            sS.wait(s)
            if last_bias is not None:
                for dt in range(NDT):
                    s.wait_ge(sV.h, last_bias[dt])
                    dSt.inc(s.dma_start(out.ap()[:, dt * T:(dt + 1) * T], rTc(dt)), 16)
            else:
                sV.wait(s)
                dSt.inc(s.dma_start(out.ap()[:, :], rT[:, :]), 16)
            s.wait_ge(dSt.h, dSt.n)

    return nc



def _prep_inputs(L0_attn_inputs, L0_attn_w_q, L0_attn_w_k, L0_attn_w_v, L0_attn_w_o,
                 L0_attn_b_o, L0_ffn_router_weight, L0_ffn_w_in, L0_ffn_w_out, L0_ffn_bias):
    f = np.float32
    x = np.ascontiguousarray(np.asarray(L0_attn_inputs, f)).reshape(8 * T, D)
    w_q = np.asarray(L0_attn_w_q, f); w_k = np.asarray(L0_attn_w_k, f)
    w_v = np.asarray(L0_attn_w_v, f); w_o = np.asarray(L0_attn_w_o, f)
    b_o = np.asarray(L0_attn_b_o, f)
    rwt = np.asarray(L0_ffn_router_weight, f)
    w_in = np.asarray(L0_ffn_w_in, f); w_out_ = np.asarray(L0_ffn_w_out, f)
    bias = np.asarray(L0_ffn_bias, f)

    perm = np.concatenate([np.concatenate([np.arange(jc * 128, (jc + 1) * 128),
                                           np.arange(4096 + jc * 128, 4096 + (jc + 1) * 128)])
                           for jc in range(32)])
    win_h = np.ascontiguousarray(
        w_in[:, perm, :].reshape(NE, NSL, 256, DC, 128).transpose(0, 1, 4, 3, 2)
        .reshape(NE, NSL, 128, 4096))
    woutT = w_out_.transpose(0, 2, 1)
    wout_h = np.ascontiguousarray(
        woutT.reshape(NE, 2, 16, 128, NDT, 128).transpose(0, 4, 1, 3, 2, 5)
        .reshape(NE, NDT, 2, 128, 2048))

    def byhead(w):
        return np.ascontiguousarray(
            w.T.reshape(DC, 128, 16, 128).transpose(2, 1, 0, 3).reshape(16, 128, 2048))

    wq_all = byhead(w_q); wk_all = byhead(w_k); wv_all = byhead(w_v)
    wo_h = np.ascontiguousarray(
        w_o.T.reshape(DC, 128, NDT, 128).transpose(2, 1, 0, 3).reshape(NDT, 128, 2048))
    rw_h = np.ascontiguousarray(rwt.T.reshape(DC, 128, 8).transpose(1, 0, 2).reshape(128, DC * 8))
    bias_h = np.ascontiguousarray(bias.reshape(16, 128).T)
    bo_h = np.ascontiguousarray(b_o.reshape(16, 128).T)
    tri_h = np.ascontiguousarray(np.triu(np.ones((128, 128), f)))
    id_h = np.ascontiguousarray(np.eye(128, dtype=f))
    ones_h = np.zeros((128, 8), f); ones_h[:, 0] = 1.0

    in_maps = []
    for c in range(8):
        xc = np.ascontiguousarray(
            x[c * T:(c + 1) * T, :].T.reshape(DC, 128, T).transpose(1, 0, 2).reshape(128, DC * T))
        consts = np.zeros((128, NCONST), f)
        consts[:, C_HALF] = 0.5
        consts[:, C_ONE] = 1.0
        consts[:, C_EPS] = 1e-5
        consts[:, C_INVD] = 1.0 / D
        consts[:, C_NEG1] = -1.0
        consts[:, C_NEGBIG] = -1e9
        consts[:, C_NEGSHIFT] = -SHIFT
        consts[:, C_S0] = 1.0 if c % 2 == 0 else 0.0
        consts[:, C_S1] = 0.0 if c % 2 == 0 else 1.0
        hsel = slice(0, 8) if c % 2 == 0 else slice(8, 16)
        in_maps.append({
            "xin": xc,
            "win": win_h,
            "wout": wout_h,
            "wq": np.ascontiguousarray(wq_all[hsel]),
            "wk": np.ascontiguousarray(wk_all[hsel]),
            "wv": np.ascontiguousarray(wv_all[hsel]),
            "wo": wo_h,
            "rw": rw_h,
            "biasH": bias_h,
            "boH": bo_h,
            "constsH": consts,
            "triH": tri_h,
            "identH": id_h,
            "onesH": ones_h,
        })
    return in_maps


def kernel(**inputs):
    in_maps = _prep_inputs(**inputs)
    nc = build()
    res = run_bass_kernel_spmd(nc, in_maps, core_ids=list(range(8)))
    parts = []
    for c in range(8):
        arr = np.asarray(res.results[c]["out"]).reshape(128, DC, T)
        parts.append(arr.transpose(1, 0, 2).reshape(D, T).T)
    full = np.concatenate(parts, axis=0).reshape(4, 2048, D)
    return np.ascontiguousarray(full.astype(np.float32))
